# version 02
# baseline (speedup 1.0000x reference)
import contextlib
import numpy as np
import ml_dtypes
import concourse.bass as bass
import concourse.mybir as mybir
from concourse.bass_utils import run_bass_kernel_spmd

F32 = mybir.dt.float32
BF16 = mybir.dt.bfloat16
I32 = mybir.dt.int32
AF = mybir.ActivationFunctionType
ALU = mybir.AluOpType
BF = ml_dtypes.bfloat16

NC = 8
DEBUG_L0PRE = False
DEBUG_MOE = False
TOK = 4096
TW = 512
NT = TOK // TW
DM = 1024
DI = 2048
S16 = 16
CH = 256
C0 = float(np.log(np.float32(1e-10)))


class Prog:
    ENGINES = ("sync", "vector", "scalar", "tensor", "gpsimd")

    def __init__(self):
        self.ops = {e: [] for e in self.ENGINES}
        self.cnt = {}

    def wait(self, eng, semname, count=None):
        c = self.cnt.get(semname, 0) if count is None else count
        if c > 0:
            self.ops[eng].append(
                lambda h, sems, _s=semname, _c=c: h.wait_ge(sems[_s], _c))
        return c

    def count(self, semname):
        return self.cnt.get(semname, 0)

    def dma(self, eng, out, in_, sem, **kw):
        def fn(h, sems, _o=out, _i=in_, _s=sem, _kw=kw):
            h.dma_start(out=_o, in_=_i, **_kw).then_inc(sems[_s], 16)
        self.ops[eng].append(fn)
        self.cnt[sem] = self.cnt.get(sem, 0) + 16
        return self.cnt[sem]

    def op(self, eng, meth, *args, sem=None, **kw):
        def fn(h, sems, _m=meth, _a=args, _kw=kw, _s=sem):
            ins = getattr(h, _m)(*_a, **_kw)
            if _s is not None:
                ins.then_inc(sems[_s], 1)
        self.ops[eng].append(fn)
        if sem is not None:
            self.cnt[sem] = self.cnt.get(sem, 0) + 1
            return self.cnt[sem]
        return None

    def run(self, block, sems):
        for eng in self.ENGINES:
            deco = getattr(block, eng)
            ops = self.ops[eng]

            def body(handle, _ops=ops):
                for f in _ops:
                    f(handle, sems)
            deco(body)


SEMNAMES = ["sx", "sw", "sv", "sa", "st", "so"]


def finish(nc, p):
    p.wait("gpsimd", "so")
    sems = {}
    with contextlib.ExitStack() as stack:
        for sn in SEMNAMES:
            sems[sn] = stack.enter_context(nc.semaphore(sn))
        with nc.Block() as block:
            p.run(block, sems)


def ap_of(t, offset, dims):
    return bass.AP(tensor=t.tensor, offset=offset, ap=dims)


def build_l0pre():
    nc = bass.Bass("TRN2", target_bir_lowering=False)
    x = nc.dram_tensor("x", [DM, TW + TOK], F32, kind="ExternalInput").ap()
    w_in = nc.dram_tensor("w_in", [DM, 2 * DI], F32, kind="ExternalInput").ap()
    w_xp = nc.dram_tensor("w_xp", [DI, 80], F32, kind="ExternalInput").ap()
    convw = nc.dram_tensor("convw", [128, 64], F32, kind="ExternalInput").ap()
    convb = nc.dram_tensor("convb", [128, 16], F32, kind="ExternalInput").ap()
    dtw = nc.dram_tensor("dtw", [128, 16], F32, kind="ExternalInput").ap()
    dtb = nc.dram_tensor("dtb", [128, 16], F32, kind="ExternalInput").ap()

    o_xn = nc.dram_tensor("xn0", [DM, TOK], BF16, kind="ExternalOutput").ap()
    o_xc = nc.dram_tensor("xc", [DI, TOK], BF16, kind="ExternalOutput").ap()
    o_z = nc.dram_tensor("z", [DI, TOK], BF16, kind="ExternalOutput").ap()
    o_dl = nc.dram_tensor("delta", [DI, TOK], F32, kind="ExternalOutput").ap()
    o_B = nc.dram_tensor("Bmat", [S16, TOK], F32, kind="ExternalOutput").ap()
    o_C = nc.dram_tensor("Cmat", [S16, TOK], F32, kind="ExternalOutput").ap()
    dbg = nc.dram_tensor("dbg", [128, 16 * (TW + 3)], BF16, kind="ExternalOutput").ap() \
        if DEBUG_L0PRE else None

    p = Prog()
    with contextlib.ExitStack() as st_:
        def sb(name, shape, dt=F32):
            return st_.enter_context(nc.sbuf_tensor(name, shape, dt))

        def psum(name, shape):
            return st_.enter_context(nc.psum_tensor(name, shape, F32))

        wi = sb("wi", [128, 8 * 4096], BF16)
        wxp = sb("wxp", [128, 16 * 80], BF16)
        cw = sb("cw", [128, 64])
        cb = sb("cb", [128, 16])
        dw = sb("dw", [128, 16])
        db = sb("db", [128, 16])
        ones_c = sb("ones_c", [128, 1])
        ones_r = sb("ones_r", [1, 128])
        epsc = sb("epsc", [128, 1])
        xt = sb("xt", [128, 2 * 8 * TW])
        sq = sb("sq", [128, 2 * TW])
        srow = sb("srow", [1, 4 * TW])
        xn = sb("xn", [128, 8 * TW], BF16)
        xnf = sb("xnf", [128, TW])
        xib = sb("xib", [128, 16 * (TW + 3)], BF16)
        zbuf = sb("zbuf", [128, 2 * TW], BF16)
        cacc = sb("cacc", [128, 2 * TW])
        xcb = sb("xcb", [128, 16 * TW], BF16)
        xp0 = sb("xp0", [1, TW])
        bst = sb("bst", [16, TW])
        cst = sb("cst", [16, TW])
        darg = sb("darg", [128, 2 * TW])
        dsp = sb("dsp", [128, 2 * TW])
        dout = sb("dout", [128, 2 * TW])

        pmm = psum("pmm", [128, 3 * 512])
        pln = psum("pln", [128, 2 * 512])
        pbc = psum("pbc", [128, 2 * 512])
        pxp = psum("pxp", [128, 512])

        for k in range(8):
            p.dma("gpsimd", wi[:, k * 4096:(k + 1) * 4096],
                  w_in[k * 128:(k + 1) * 128, :], "sw")
        for k in range(16):
            p.dma("gpsimd", wxp[:, k * 80:(k + 1) * 80],
                  w_xp[k * 128:(k + 1) * 128, :], "sw")
        p.dma("sync", cw[:], convw[:], "sx")
        p.dma("sync", cb[:], convb[:], "sx")
        p.dma("sync", dw[:], dtw[:], "sx")
        p.dma("sync", db[:], dtb[:], "sx")
        p.op("vector", "memset", ones_c[:], 1.0)
        p.op("vector", "memset", epsc[:], 1e-5)
        p.op("vector", "memset", ones_r[:], 1.0, sem="sv")
        wload = p.count("sw")
        static_sx = p.count("sx")
        memset_sv = p.count("sv")
        p.wait("vector", "sx", static_sx)

        bank_evac = {b: 0 for b in range(3)}
        xt_free_sv = [0, 0]
        sq_free_st = [0, 0]
        z_free_so = [0, 0]
        d_free_so = [0, 0]
        d_sa_prev = [0, 0]
        d_sv_prev = [0, 0]
        silu_sa_prev = [0, 0]
        bc_free_so = 0
        carry_sv = 0
        xn_free_st = 0
        xn_free_so = 0
        pln_free_sv = 0
        pxp_free_sv = 0

        tiles = [(-1, TW)] + [(t, TW) for t in range(NT)]
        for (t, W) in tiles:
            ti = t + 1
            buf = ti % 2
            xcol0 = 0 if t < 0 else TW + t * TW

            p.wait("sync", "sv", xt_free_sv[buf])
            src = ap_of(x, xcol0, [[TW + TOK, 128], [128 * (TW + TOK), 8], [1, W]])
            xv = xt[:, buf * 8 * TW: buf * 8 * TW + 8 * W].rearrange(
                "p (k w) -> p k w", k=8)
            sxc = p.dma("sync", xv, src, "sx")

            def xk_(k):
                return xt[:, buf * 8 * TW + k * W: buf * 8 * TW + (k + 1) * W]

            p.wait("scalar", "sx", sxc)
            p.wait("tensor", "sx", sxc)
            if ti == 0:
                p.wait("tensor", "sv", memset_sv)
            p.wait("tensor", "sv", pln_free_sv)
            for k in range(8):
                sqb = sq[:, (k % 2) * TW: (k % 2) * TW + W]
                p.wait("scalar", "st", sq_free_st[k % 2])
                sa_sq = p.op("scalar", "activation", sqb, xk_(k), AF.Square, sem="sa")
                p.op("tensor", "matmul", pln[0:1, 0:W], ones_c[:], xk_(k),
                     start=(k == 0), stop=(k == 7))
                p.wait("tensor", "sa", sa_sq)
                sq_free_st[k % 2] = p.op(
                    "tensor", "matmul", pln[0:1, 512:512 + W], ones_c[:], sqb,
                    start=(k == 0), stop=(k == 7), sem="st")
            ln_mm_done = p.count("st")
            p.wait("vector", "st", ln_mm_done)
            r0 = srow[0:1, 0:W]
            r1 = srow[0:1, TW:TW + W]
            r2 = srow[0:1, 2 * TW:2 * TW + W]
            r3 = srow[0:1, 3 * TW:3 * TW + W]
            p.op("vector", "tensor_scalar", r0, pln[0:1, 0:W], 1.0 / DM, None, ALU.mult)
            pln_free_sv = p.op("vector", "tensor_scalar", r1, pln[0:1, 512:512 + W],
                               1.0 / DM, None, ALU.mult, sem="sv")
            p.op("vector", "tensor_mul", r3, r0, r0)
            sv_var = p.op("vector", "tensor_sub", r2, r1, r3, sem="sv")
            p.wait("scalar", "sv", sv_var)
            if ti == 0:
                p.wait("scalar", "sv", memset_sv)
            p.op("scalar", "activation", r2, r2, AF.Ln, bias=epsc[0:1, 0:1])
            sa_rstd = p.op("scalar", "activation", r2, r2, AF.Exp, scale=-0.5, sem="sa")
            p.wait("tensor", "sa", sa_rstd)
            p.op("tensor", "matmul", pbc[:, 0:W], ones_r[:], r0, start=True, stop=True)
            st_bc = p.op("tensor", "matmul", pbc[:, 512:512 + W], ones_r[:], r2,
                         start=True, stop=True, sem="st")
            p.wait("vector", "st", st_bc)
            p.wait("vector", "st", xn_free_st)
            p.wait("vector", "so", xn_free_so)
            for k in range(8):
                p.op("vector", "tensor_sub", xnf[:, 0:W], xk_(k), pbc[:, 0:W])
                sv_xn = p.op("vector", "tensor_mul", xn[:, k * TW:k * TW + W],
                             xnf[:, 0:W], pbc[:, 512:512 + W], sem="sv")
            xt_free_sv[buf] = sv_xn
            if t >= 0:
                p.wait("gpsimd", "sv", sv_xn)
                dst = ap_of(o_xn, t * TW, [[TOK, 128], [128 * TOK, 8], [1, W]])
                xn_free_so = p.dma(
                    "gpsimd", dst,
                    xn[:].rearrange("p (k w) -> p k w", k=8)[:, :, 0:W], "so")

            p.wait("tensor", "sv", sv_xn)
            if ti == 0:
                p.wait("tensor", "sw", wload)
            mlist = range(16) if t < 0 else range(32)
            for m in mlist:
                b = m % 3
                bank = pmm[:, b * 512: b * 512 + W]
                p.wait("tensor", "sa", bank_evac[b])
                for k in range(8):
                    p.op("tensor", "matmul", bank,
                         wi[:, k * 4096 + m * 128: k * 4096 + (m + 1) * 128],
                         xn[:, k * TW:k * TW + W],
                         start=(k == 0), stop=(k == 7),
                         sem=("st" if k == 7 else None))
                mm_done = p.count("st")
                p.wait("scalar", "st", mm_done)
                if m < 16:
                    if t < 0:
                        evac_dst = xib[:, m * (TW + 3): m * (TW + 3) + 3]
                        evac_src = pmm[:, b * 512 + W - 3: b * 512 + W]
                    else:
                        p.wait("scalar", "sv", carry_sv)
                        evac_dst = xib[:, m * (TW + 3) + 3: m * (TW + 3) + 3 + W]
                        evac_src = bank
                    bank_evac[b] = p.op("scalar", "copy", evac_dst, evac_src, sem="sa")
                else:
                    zv = zbuf[:, (m % 2) * TW:(m % 2) * TW + W]
                    p.wait("scalar", "so", z_free_so[m % 2])
                    bank_evac[b] = p.op("scalar", "copy", zv, bank, sem="sa")
                    p.wait("gpsimd", "sa", bank_evac[b])
                    dstz = ap_of(o_z, (m - 16) * 128 * TOK + t * TW, [[TOK, 128], [1, W]])
                    z_free_so[m % 2] = p.dma("gpsimd", dstz, zv, "so")
            xn_free_st = p.count("st")

            if t < 0:
                if DEBUG_L0PRE:
                    p.wait("gpsimd", "sa", p.count("sa"))
                    p.dma("gpsimd", dbg[:], xib[:], "so")
                continue

            evacs_done = p.count("sa")
            p.wait("vector", "sa", evacs_done)
            if DEBUG_L0PRE:
                p.wait("vector", "so", p.count("so"))
            for m in range(16):
                base = m * (TW + 3)
                cbuf = m % 2
                cv = cacc[:, cbuf * TW:cbuf * TW + W]
                p.wait("vector", "sa", silu_sa_prev[cbuf])
                p.op("vector", "tensor_scalar", cv, xib[:, base: base + W],
                     cw[:, 4 * m:4 * m + 1], None, ALU.mult)
                for k in (1, 2):
                    p.op("vector", "scalar_tensor_tensor", cv,
                         xib[:, base + k: base + k + W],
                         cw[:, 4 * m + k:4 * m + k + 1], cv, ALU.mult, ALU.add)
                sv_c = p.op("vector", "scalar_tensor_tensor", cv,
                            xib[:, base + 3: base + 3 + W],
                            cw[:, 4 * m + 3:4 * m + 4], cv, ALU.mult, ALU.add, sem="sv")
                p.wait("scalar", "sv", sv_c)
                silu_sa_prev[cbuf] = p.op(
                    "scalar", "activation", xcb[:, m * TW:m * TW + W], cv,
                    AF.Silu, bias=cb[:, m:m + 1], sem="sa")
                carry_sv = p.op("vector", "tensor_copy", xib[:, base:base + 3],
                                xib[:, base + W:base + W + 3], sem="sv")
            silu_done = p.count("sa")
            p.wait("gpsimd", "sa", silu_done)
            dstxc = ap_of(o_xc, t * TW, [[TOK, 128], [128 * TOK, 16], [1, W]])
            p.dma("gpsimd", dstxc,
                  xcb[:].rearrange("p (k w) -> p k w", k=16)[:, :, 0:W], "so")

            p.wait("tensor", "sa", silu_done)
            p.wait("tensor", "sv", pxp_free_sv)
            for k in range(16):
                p.op("tensor", "matmul", pxp[0:80, 0:W], wxp[:, k * 80:(k + 1) * 80],
                     xcb[:, k * TW:k * TW + W], start=(k == 0), stop=(k == 15),
                     sem=("st" if k == 15 else None))
            xp_done = p.count("st")
            p.wait("scalar", "st", xp_done)
            p.wait("scalar", "so", bc_free_so)
            p.op("scalar", "copy", bst[:, 0:W], pxp[32:48, 0:W])
            p.op("scalar", "copy", cst[:, 0:W], pxp[64:80, 0:W])
            sa_xp = p.op("scalar", "copy", xp0[0:1, 0:W], pxp[0:1, 0:W], sem="sa")
            p.wait("gpsimd", "sa", sa_xp)
            p.dma("gpsimd", ap_of(o_B, t * TW, [[TOK, 16], [1, W]]), bst[:, 0:W], "so")
            bc_free_so = p.dma("gpsimd", ap_of(o_C, t * TW, [[TOK, 16], [1, W]]),
                               cst[:, 0:W], "so")
            p.wait("tensor", "sa", sa_xp)
            st_d0 = p.op("tensor", "matmul", pxp[:, 0:W], ones_r[:], xp0[0:1, 0:W],
                         start=True, stop=True, sem="st")

            p.wait("vector", "st", st_d0)
            for m in range(16):
                dbuf = m % 2
                da = darg[:, dbuf * TW:dbuf * TW + W]
                ds = dsp[:, dbuf * TW:dbuf * TW + W]
                do = dout[:, dbuf * TW:dbuf * TW + W]
                p.wait("vector", "so", d_free_so[dbuf])
                p.wait("vector", "sa", d_sa_prev[dbuf])
                sv_arg = p.op("vector", "tensor_scalar", da, pxp[:, 0:W],
                              dw[:, m:m + 1], db[:, m:m + 1], ALU.mult, ALU.add,
                              sem="sv")
                p.wait("scalar", "sv", sv_arg)
                p.wait("scalar", "sv", d_sv_prev[dbuf])
                p.op("scalar", "activation", ds, da, AF.Abs)
                p.op("scalar", "activation", ds, ds, AF.Exp, scale=-1.0)
                d_sa_prev[dbuf] = p.op("scalar", "activation", ds, ds, AF.Ln,
                                       bias=1.0, sem="sa")
                p.wait("vector", "sa", d_sa_prev[dbuf])
                d_sv_prev[dbuf] = p.op("vector", "scalar_tensor_tensor", do, da, 0.0,
                                       ds, ALU.max, ALU.add, sem="sv")
                p.wait("gpsimd", "sv", d_sv_prev[dbuf])
                dstd = ap_of(o_dl, m * 128 * TOK + t * TW, [[TOK, 128], [1, W]])
                d_free_so[dbuf] = p.dma("gpsimd", dstd, do, "so")
            pxp_free_sv = p.count("sv")

        finish(nc, p)
    return nc


TT32 = 32768
NCHTOT = TT32 // CH


def build_scan():
    nc = bass.Bass("TRN2", target_bir_lowering=False)
    dl = nc.dram_tensor("delta", [256, TT32], F32, kind="ExternalInput").ap()
    xc = nc.dram_tensor("xc", [256, TT32], BF16, kind="ExternalInput").ap()
    Bs = nc.dram_tensor("Bs", [NCHTOT, S16 * CH], F32, kind="ExternalInput").ap()
    Cs = nc.dram_tensor("Cs", [NCHTOT, S16 * CH], F32, kind="ExternalInput").ap()
    Alog = nc.dram_tensor("Alog", [1, S16], F32, kind="ExternalInput").ap()
    Dv = nc.dram_tensor("Dv", [128, 2], F32, kind="ExternalInput").ap()

    o_y = nc.dram_tensor("y", [256, TT32], BF16, kind="ExternalOutput").ap()

    p = Prog()
    with contextlib.ExitStack() as st_:
        def sb(name, shape, dt=F32):
            return st_.enter_context(nc.sbuf_tensor(name, shape, dt))

        arow = sb("arow", [1, S16])
        abc = sb("abc", [128, S16])
        dvt = sb("dvt", [128, 2])
        ones_r = sb("ones_r", [1, 128])
        zf = sb("zf", [128, CH])
        zb = sb("zb", [128, CH], BF16)
        dlt = sb("dlt", [128, 2 * 2 * CH])
        xct = sb("xct", [128, 2 * 2 * CH], BF16)
        Brep = sb("Brep", [128, 2 * S16 * CH], BF16)
        Crep = sb("Crep", [128, 2 * S16 * CH], BF16)
        Ebuf = sb("Ebuf", [128, CH + 1])
        As = sb("As", [128, 2 * S16 * (CH + 1)], BF16)
        tmp = sb("tmp", [128, 2 * S16 * CH])
        rho = sb("rho", [128, 2 * S16 * CH], BF16)
        dxb = sb("dxb", [128, 2 * CH], BF16)
        bxf = sb("bxf", [128, S16 * CH], BF16)
        t2 = sb("t2", [128, S16 * CH], BF16)
        hA = sb("hA", [128, S16 * CH], BF16)
        h2 = sb("h2", [128, S16 * CH], BF16)
        hb = sb("hb", [128, S16 * CH], BF16)
        Pb = sb("Pb", [128, S16 * CH], BF16)
        tr1 = sb("tr1", [128, 8 * CH], BF16)
        tr2 = sb("tr2", [128, 4 * CH], BF16)
        tr3 = sb("tr3", [128, 2 * CH], BF16)
        ysc = sb("ysc", [128, CH])
        yout = sb("yout", [128, 2 * CH], BF16)
        hst = sb("hst", [128, 2 * S16], BF16)
        pA = st_.enter_context(nc.psum_tensor("pA", [128, 512], F32))

        p.dma("sync", arow[:], Alog[:], "sx")
        p.dma("sync", dvt[:], Dv[:], "sx")
        st_sx = p.count("sx")
        p.op("vector", "memset", ones_r[:], 1.0)
        p.op("vector", "memset", zf[:], 0.0)
        p.op("vector", "memset", zb[:], 0.0)
        p.op("vector", "memset", Ebuf[:, 0:1], 0.0)
        asv2 = As[:].rearrange("p (u s l) -> p u s l", u=2, l=CH + 1)
        p.op("vector", "memset", asv2[:, :, :, 0:1], 1.0, sem="sv")
        ms_sv = p.count("sv")
        p.wait("scalar", "sx", st_sx)
        sa_a = p.op("scalar", "activation", arow[:], arow[:], AF.Exp, scale=1.0, sem="sa")
        p.wait("vector", "sa", sa_a)
        sv_a = p.op("vector", "tensor_scalar", arow[:], arow[:], -1.0, None, ALU.mult,
                    sem="sv")
        p.wait("tensor", "sv", sv_a)
        st_a = p.op("tensor", "matmul", pA[:, 0:S16], ones_r[:], arow[:],
                    start=True, stop=True, sem="st")
        p.wait("scalar", "st", st_a)
        sa_abc = p.op("scalar", "copy", abc[:], pA[:, 0:S16], sem="sa")

        in_free_sv = [0, 0]
        in_free_sa = [0, 0]
        bc_free_sv = [0, 0]
        y_free_so = [0, 0]
        as_free_sv = [0, 0]
        tmp_free_sa = [0, 0]
        sw_bc_arr = {}
        sx_in_arr = {}

        iters = []
        for b in range(4):
            for c in range(32):
                for dt in range(2):
                    iters.append((b, c, dt))

        def emitA(i):
            b, c, dt = iters[i]
            ci = b * 32 + c
            buf = ci % 2
            pp = i % 2
            col0 = b * 8192 + c * CH
            if dt == 0:
                p.wait("sync", "sv", in_free_sv[buf])
                p.wait("sync", "sa", in_free_sa[buf])
                for d2 in range(2):
                    off = (buf * 2 + d2) * CH
                    p.dma("sync", dlt[:, off:off + CH],
                          dl[d2 * 128:(d2 + 1) * 128, col0:col0 + CH], "sx")
                    p.dma("sync", xct[:, off:off + CH],
                          xc[d2 * 128:(d2 + 1) * 128, col0:col0 + CH], "sx")
                sx_in_arr[ci] = p.count("sx")
                p.wait("gpsimd", "sv", bc_free_sv[buf])
                boff = buf * S16 * CH
                p.dma("gpsimd", Brep[:, boff:boff + S16 * CH],
                      ap_of(Bs, ci * S16 * CH, [[0, 128], [1, S16 * CH]]), "sw")
                sw_bc_arr[ci] = p.dma(
                    "gpsimd", Crep[:, boff:boff + S16 * CH],
                    ap_of(Cs, ci * S16 * CH, [[0, 128], [1, S16 * CH]]), "sw")
            off = (buf * 2 + dt) * CH
            dchunk = dlt[:, off:off + CH]
            xchunk = xct[:, off:off + CH]
            tmpp = tmp[:, pp * S16 * CH:(pp + 1) * S16 * CH]
            dxp = dxb[:, pp * CH:(pp + 1) * CH]
            p.wait("vector", "sx", sx_in_arr[ci])
            if i == 0:
                p.wait("vector", "sa", sa_abc)
            p.op("vector", "tensor_tensor_scan", Ebuf[:, 1:CH + 1],
                 dchunk, zf[:], 0.0, ALU.add, ALU.add)
            p.op("vector", "tensor_mul", dxp, dchunk, xchunk)
            p.wait("vector", "sa", tmp_free_sa[pp])
            for s in range(S16):
                p.op("vector", "tensor_scalar", tmpp[:, s * CH:(s + 1) * CH],
                     Ebuf[:, 0:CH], abc[:, s:s + 1], C0, ALU.mult, ALU.subtract)
            return p.op("vector", "engine_nop", sem="sv")

        def emitB(i, sv_tmp):
            b, c, dt = iters[i]
            ci = b * 32 + c
            buf = ci % 2
            pp = i % 2
            off = (buf * 2 + dt) * CH
            dchunk = dlt[:, off:off + CH]
            asv = asv2[:, pp]
            rhop = rho[:, pp * S16 * CH:(pp + 1) * S16 * CH]
            tmpp = tmp[:, pp * S16 * CH:(pp + 1) * S16 * CH]
            p.wait("scalar", "sv", sv_tmp)
            p.wait("scalar", "sv", as_free_sv[pp])
            for s in range(S16):
                p.op("scalar", "activation", asv[:, s, 1:CH + 1], dchunk,
                     AF.Exp, scale=abc[:, s:s + 1])
            for s in range(S16):
                p.op("scalar", "activation", rhop[:, s * CH:(s + 1) * CH],
                     tmpp[:, s * CH:(s + 1) * CH], AF.Exp,
                     sem=("sa" if s == S16 - 1 else None))
            tmp_free_sa[pp] = p.count("sa")
            return tmp_free_sa[pp]

        def emitC(i, sa_acts):
            b, c, dt = iters[i]
            ci = b * 32 + c
            buf = ci % 2
            pp = i % 2
            col0 = b * 8192 + c * CH
            boff = buf * S16 * CH
            off = (buf * 2 + dt) * CH
            xchunk = xct[:, off:off + CH]
            hcol = hst[:, dt * S16:(dt + 1) * S16]
            asv = asv2[:, pp]
            rhop = rho[:, pp * S16 * CH:(pp + 1) * S16 * CH]
            dxp = dxb[:, pp * CH:(pp + 1) * CH]
            if c == 0:
                p.op("vector", "memset", hcol, 0.0)
            p.wait("vector", "sw", sw_bc_arr[ci])
            dxbc = dxp.unsqueeze(1).broadcast_to([128, S16, CH])
            p.op("vector", "tensor_tensor",
                 bxf[:].rearrange("p (s l) -> p s l", l=CH), dxbc,
                 Brep[:, boff:boff + S16 * CH].rearrange(
                     "p (s l) -> p s l", l=CH), ALU.mult)
            p.wait("vector", "sa", sa_acts)
            p.op("vector", "scalar_tensor_tensor", t2[:], rhop[:], 1.0,
                 bxf[:], ALU.min, ALU.mult)
            for s in range(S16):
                p.op("vector", "tensor_tensor_scan",
                     hA[:, s * CH:(s + 1) * CH], asv[:, s, 0:CH],
                     t2[:, s * CH:(s + 1) * CH], 0.0, ALU.mult, ALU.add)
            for s in range(S16):
                p.op("vector", "tensor_tensor_scan",
                     h2[:, s * CH:(s + 1) * CH], asv[:, s, 1:CH + 1],
                     zb[:], hcol[:, s:s + 1], ALU.mult, ALU.add)
            as_free_sv[pp] = p.op("vector", "engine_nop", sem="sv")
            p.op("vector", "tensor_add", hb[:], hA[:], h2[:])
            hbv = hb[:].rearrange("p (s l) -> p s l", l=CH)
            p.op("vector", "tensor_copy", hcol,
                 hbv[:, :, CH - 1:CH].rearrange("p s one -> p (s one)"))
            p.op("vector", "tensor_mul", Pb[:], hb[:],
                 Crep[:, boff:boff + S16 * CH])
            bc_free_sv[buf] = p.op("vector", "engine_nop", sem="sv")
            p.op("vector", "tensor_add", tr1[:], Pb[:, 0:8 * CH], Pb[:, 8 * CH:])
            p.op("vector", "tensor_add", tr2[:], tr1[:, 0:4 * CH], tr1[:, 4 * CH:])
            p.op("vector", "tensor_add", tr3[:], tr2[:, 0:2 * CH], tr2[:, 2 * CH:])
            p.op("vector", "tensor_add", ysc[:], tr3[:, 0:CH], tr3[:, CH:])
            ybv = yout[:, dt * CH:(dt + 1) * CH]
            p.wait("vector", "so", y_free_so[dt])
            sv_y = p.op("vector", "scalar_tensor_tensor", ybv, xchunk,
                        dvt[:, dt:dt + 1], ysc[:], ALU.mult, ALU.add, sem="sv")
            in_free_sv[buf] = sv_y
            in_free_sa[buf] = tmp_free_sa[(i) % 2]
            p.wait("gpsimd", "sv", sv_y)
            y_free_so[dt] = p.dma(
                "gpsimd", o_y[dt * 128:(dt + 1) * 128, col0:col0 + CH],
                ybv, "so")

        n = len(iters)
        pend = {}
        for i in range(n + 1):
            if i < n:
                svt = emitA(i)
                pend[i] = emitB(i, svt)
            if i > 0:
                emitC(i - 1, pend.pop(i - 1))
        finish(nc, p)
    return nc


def run_scan(inputs, delta, xcv, Bp, Cp):
    Alog = inputs["L0_attn_A_log"].reshape(1, S16).astype(np.float32)
    Dp = inputs["L0_attn_D_param"].astype(np.float32)
    Bsm = np.ascontiguousarray(
        Bp.reshape(NCHTOT, CH, S16).transpose(0, 2, 1).reshape(NCHTOT, S16 * CH))
    Csm = np.ascontiguousarray(
        Cp.reshape(NCHTOT, CH, S16).transpose(0, 2, 1).reshape(NCHTOT, S16 * CH))
    nc = _get("scan", build_scan)
    in_maps = []
    for core in range(NC):
        d0 = core * 256
        in_maps.append({
            "delta": np.ascontiguousarray(delta[:, d0:d0 + 256].T),
            "xc": np.ascontiguousarray(xcv[:, d0:d0 + 256].T),
            "Bs": Bsm, "Cs": Csm, "Alog": Alog,
            "Dv": np.ascontiguousarray(Dp[d0:d0 + 256].reshape(2, 128).T),
        })
    res = run_bass_kernel_spmd(nc, in_maps, core_ids=list(range(NC))).results
    y = np.empty((TT32, DI), dtype=BF)
    for core in range(NC):
        y[:, core * 256:(core + 1) * 256] = res[core]["y"].T
    return y


def emit_ln_fm(p, nk, W, srcs, psq, pbx, sqbuf, rows, ones_c, ones_r, eps_col,
               dmw, sq_free_st, pln_free_sv_ref, pbx_free_sv=0):
    p.wait("tensor", "sv", pln_free_sv_ref[0])
    p.wait("tensor", "sv", pbx_free_sv)
    for k in range(nk):
        sqb = sqbuf[:, (k % 2) * W:(k % 2) * W + W]
        p.wait("scalar", "st", sq_free_st[k % 2])
        sa_sq = p.op("scalar", "activation", sqb, srcs[k], AF.Square, sem="sa")
        p.op("tensor", "matmul", psq[0:1, 0:W], ones_c[:], srcs[k],
             start=(k == 0), stop=(k == nk - 1))
        p.wait("tensor", "sa", sa_sq)
        sq_free_st[k % 2] = p.op("tensor", "matmul", psq[0:1, 512:512 + W],
                                 ones_c[:], sqb, start=(k == 0), stop=(k == nk - 1),
                                 sem="st")
    p.wait("vector", "st", p.count("st"))
    r0 = rows[0:1, 0:W]
    r1 = rows[0:1, W:2 * W]
    r2 = rows[0:1, 2 * W:3 * W]
    r3 = rows[0:1, 3 * W:4 * W]
    p.op("vector", "tensor_scalar", r0, psq[0:1, 0:W], 1.0 / dmw, None, ALU.mult)
    pln_free_sv_ref[0] = p.op("vector", "tensor_scalar", r1, psq[0:1, 512:512 + W],
                              1.0 / dmw, None, ALU.mult, sem="sv")
    p.op("vector", "tensor_mul", r3, r0, r0)
    sv_var = p.op("vector", "tensor_sub", r2, r1, r3, sem="sv")
    p.wait("scalar", "sv", sv_var)
    p.op("scalar", "activation", r2, r2, AF.Ln, bias=eps_col)
    sa_rstd = p.op("scalar", "activation", r2, r2, AF.Exp, scale=-0.5, sem="sa")
    p.wait("tensor", "sa", sa_rstd)
    p.op("tensor", "matmul", pbx[:, 0:W], ones_r[:], r0, start=True, stop=True)
    st_bc = p.op("tensor", "matmul", pbx[:, 512:512 + W], ones_r[:], r2,
                 start=True, stop=True, sem="st")
    return pbx[:, 0:W], pbx[:, 512:512 + W], st_bc


def build_wcast():
    nc = bass.Bass("TRN2", target_bir_lowering=False)
    w1t = nc.dram_tensor("w1t", [8192, 4096], F32, kind="ExternalInput").ap()
    w2t = nc.dram_tensor("w2t", [8192, 4096], F32, kind="ExternalInput").ap()
    o1 = nc.dram_tensor("w1b", [8192, 4096], BF16, kind="ExternalOutput").ap()
    o2 = nc.dram_tensor("w2b", [8192, 4096], BF16, kind="ExternalOutput").ap()
    p = Prog()
    with contextlib.ExitStack() as st_:
        buf = st_.enter_context(nc.sbuf_tensor("buf", [128, 4 * 4096], BF16))
        free_so = [0, 0, 0, 0]
        for i in range(128):
            bb = i % 4
            src = w1t if i < 64 else w2t
            dst = o1 if i < 64 else o2
            j = i % 64
            p.wait("gpsimd", "so", free_so[bb])
            sx = p.dma("gpsimd", buf[:, bb * 4096:(bb + 1) * 4096],
                       src[j * 128:(j + 1) * 128, :], "sw")
            p.wait("sync", "sw", sx)
            free_so[bb] = p.dma("sync", dst[j * 128:(j + 1) * 128, :],
                                buf[:, bb * 4096:(bb + 1) * 4096], "so")
        p.wait("sync", "so")
        finish(nc, p)
    return nc


def run_wcast(inputs):
    w1 = inputs["L0_ffn_w1"]
    w2 = inputs["L0_ffn_w2"]
    w1tt = np.ascontiguousarray(w1.transpose(0, 2, 1)).reshape(8192, 4096)
    w2tt = np.ascontiguousarray(w2.transpose(0, 2, 1)).reshape(8192, 4096)
    nc = _get("wcast", build_wcast)
    in_maps = [{"w1t": w1tt, "w2t": w2tt}] * NC
    res = run_bass_kernel_spmd(nc, in_maps, core_ids=list(range(NC))).results
    return res


def build_moe():
    nc = bass.Bass("TRN2", target_bir_lowering=False)
    xn0 = nc.dram_tensor("xn0", [DM, TOK], BF16, kind="ExternalInput").ap()
    w1t = nc.dram_tensor("w1t", [8, DM, 4096], BF16, kind="ExternalInput").ap()
    w2t = nc.dram_tensor("w2t", [8, 4096, DM], BF16, kind="ExternalInput").ap()
    b1c = nc.dram_tensor("b1c", [128, 8 * 32], F32, kind="ExternalInput").ap()
    b2c = nc.dram_tensor("b2c", [128, 8 * 8], F32, kind="ExternalInput").ap()
    rwt = nc.dram_tensor("rwt", [DM, 8], F32, kind="ExternalInput").ap()
    rbr = nc.dram_tensor("rbr", [1, 8], F32, kind="ExternalInput").ap()
    lnwc = nc.dram_tensor("lnwc", [128, 8], F32, kind="ExternalInput").ap()
    lnbc = nc.dram_tensor("lnbc", [128, 8], F32, kind="ExternalInput").ap()
    eyein = nc.dram_tensor("eyein", [128, 128], F32, kind="ExternalInput").ap()

    o_g0 = nc.dram_tensor("g0", [DM, TOK], BF16, kind="ExternalOutput").ap()
    cwdram = nc.dram_tensor("cwdram", [8, TOK], F32,
                            kind="ExternalOutput" if DEBUG_MOE else "Internal").ap()
    if DEBUG_MOE:
        o_acc = nc.dram_tensor("o_acc", [128, 8 * 512], F32, kind="ExternalOutput").ap()
        o_geb = nc.dram_tensor("o_geb", [128, 32 * 512], BF16, kind="ExternalOutput").ap()

    p = Prog()
    W = 512
    with contextlib.ExitStack() as st_:
        def sb(name, shape, dt=F32):
            return st_.enter_context(nc.sbuf_tensor(name, shape, dt))

        ones_c = sb("ones_c", [128, 1])
        ones_r = sb("ones_r", [1, 128])
        epsc = sb("epsc", [128, 1])
        rws = sb("rws", [128, 64], BF16)
        rbrow = sb("rbrow", [1, 8])
        rbrep = sb("rbrep", [128, 8])
        lnw = sb("lnw", [128, 8])
        lnb = sb("lnb", [128, 8])
        b1s = sb("b1s", [128, 256])
        b2s = sb("b2s", [128, 64])
        eye = sb("eye", [128, 128])
        xnt = sb("xnt", [128, 2 * 8 * W], BF16)
        lgt = sb("lgt", [128, 8])
        neg = sb("neg", [128, 8])
        msk = sb("msk", [128, 8])
        mrow = sb("mrow", [128, 4])
        cwtt = sb("cwtt", [128, 8])
        c2t = sb("c2t", [128, 8])
        cwT = sb("cwT", [8, TOK])
        wu = sb("wu", [128, 2 * 8 * 512], BF16)
        wd = sb("wd", [128, 2 * 32 * 256], BF16)
        geb = sb("geb", [128, 32 * W], BF16)
        acc = sb("acc", [128, 8 * W])
        cwr = sb("cwr", [128, 2 * W])
        tmpv = sb("tmpv", [128, W])
        sqbuf = sb("sqbuf", [128, 2 * W])
        rows = sb("rows", [1, 4 * W])
        gsc = sb("gsc", [128, 2 * W])
        gsc2 = sb("gsc2", [128, 2 * W])
        g0t = sb("g0t", [128, 8 * W], BF16)

        pA = st_.enter_context(nc.psum_tensor("pA", [128, 4 * 512], F32))
        pB = st_.enter_context(nc.psum_tensor("pB", [128, 4 * 512], F32))

        p.dma("gpsimd", rws[:].rearrange("p (k e) -> p k e", e=8),
              ap_of(rwt, 0, [[8, 128], [128 * 8, 8], [1, 8]]), "sx")
        p.dma("sync", rbrow[:], rbr[:], "sx")
        p.dma("sync", lnw[:], lnwc[:], "sx")
        p.dma("sync", lnb[:], lnbc[:], "sx")
        p.dma("sync", b1s[:], b1c[:], "sx")
        p.dma("sync", b2s[:], b2c[:], "sx")
        p.dma("sync", eye[:], eyein[:], "sx")
        st_sx = p.count("sx")
        p.op("vector", "memset", ones_c[:], 1.0)
        p.op("vector", "memset", ones_r[:], 1.0)
        p.op("vector", "memset", epsc[:], 1e-12, sem="sv")
        ms_sv = p.count("sv")
        p.wait("tensor", "sx", st_sx)
        p.wait("tensor", "sv", ms_sv)
        st_rb = p.op("tensor", "matmul", pA[:, 512:512 + 8], ones_r[:], rbrow[:],
                     start=True, stop=True, sem="st")
        p.wait("scalar", "st", st_rb)
        sa_rb = p.op("scalar", "copy", rbrep[:], pA[:, 512:512 + 8], sem="sa")

        xnt_free_st = [0, 0]
        xnt_free_sv = [0, 0]
        plog_free_sv = 0
        ptr_free_sa = 0
        cwtt_free_st = 0
        for t8 in range(NT):
            buf = t8 % 2
            p.wait("sync", "st", xnt_free_st[buf])
            src = ap_of(xn0, t8 * W, [[TOK, 128], [128 * TOK, 8], [1, W]])
            xv = xnt[:, buf * 8 * W:(buf + 1) * 8 * W].rearrange(
                "p (k w) -> p k w", k=8)
            sx_t = p.dma("sync", xv, src, "sx")
            p.wait("tensor", "sx", sx_t)
            for tt in range(4):
                tok0 = t8 * W + tt * 128
                p.wait("tensor", "sv", plog_free_sv)
                for k in range(8):
                    p.op("tensor", "matmul", pA[:, 0:8],
                         xnt[:, buf * 8 * W + k * W + tt * 128:
                             buf * 8 * W + k * W + (tt + 1) * 128],
                         rws[:, k * 8:(k + 1) * 8],
                         start=(k == 0), stop=(k == 7),
                         sem=("st" if k == 7 else None))
                st_lg = p.count("st")
                p.wait("vector", "st", st_lg)
                if t8 == 0 and tt == 0:
                    p.wait("vector", "sa", sa_rb)
                c = p.op("vector", "tensor_add", lgt[:], pA[:, 0:8], rbrep[:],
                         sem="sv")
                p.wait("vector", "sv", c)
                c = p.op("vector", "reduce_max", mrow[:, 0:1].unsqueeze(2),
                         lgt[:].unsqueeze(1), axis=mybir.AxisListType.X, sem="sv")
                p.wait("vector", "sv", c)
                c = p.op("vector", "tensor_scalar", neg[:], lgt[:], mrow[:, 0:1],
                         -1e30, ALU.is_equal, ALU.mult, sem="sv")
                p.wait("vector", "sv", c)
                c = p.op("vector", "tensor_add", msk[:], lgt[:], neg[:], sem="sv")
                p.wait("vector", "sv", c)
                c = p.op("vector", "reduce_max", mrow[:, 1:2].unsqueeze(2),
                         msk[:].unsqueeze(1), axis=mybir.AxisListType.X, sem="sv")
                p.wait("vector", "sv", c)
                p.op("vector", "tensor_sub", mrow[:, 2:3], mrow[:, 0:1], mrow[:, 1:2])
                sv_m = p.op("vector", "tensor_sub", mrow[:, 3:4], mrow[:, 1:2],
                            mrow[:, 0:1], sem="sv")
                p.wait("scalar", "sv", sv_m)
                p.op("scalar", "activation", mrow[:, 2:3], mrow[:, 2:3], AF.Sigmoid)
                sa_sg = p.op("scalar", "activation", mrow[:, 3:4], mrow[:, 3:4],
                             AF.Sigmoid, sem="sa")
                p.wait("vector", "sa", sa_sg)
                p.wait("vector", "st", cwtt_free_st)
                p.op("vector", "tensor_scalar", cwtt[:], lgt[:], mrow[:, 0:1],
                     mrow[:, 2:3], ALU.is_equal, ALU.mult)
                c = p.op("vector", "tensor_scalar", c2t[:], msk[:], mrow[:, 1:2],
                         mrow[:, 3:4], ALU.is_equal, ALU.mult, sem="sv")
                p.wait("vector", "sv", c)
                plog_free_sv = p.op("vector", "tensor_add", cwtt[:], cwtt[:],
                                    c2t[:], sem="sv")
                p.wait("tensor", "sv", plog_free_sv)
                p.wait("tensor", "sa", ptr_free_sa)
                cwtt_free_st = p.op("tensor", "transpose", pA[0:8, 1024:1024 + 128],
                                    cwtt[:], eye[:], sem="st")
                p.wait("scalar", "st", cwtt_free_st)
                ptr_free_sa = p.op("scalar", "copy", cwT[0:8, tok0:tok0 + 128],
                                   pA[0:8, 1024:1024 + 128], sem="sa")
            xnt_free_st[buf] = p.count("st")
        p.wait("gpsimd", "sa", ptr_free_sa)
        so_cw = p.dma("gpsimd", cwdram[:], cwT[:], "so")

        wu_free = [0, 0]
        wd_free = [0, 0]
        cwr_free = [0, 0]
        upbank_free = [0, 0]
        dnbank_free = [0, 0]
        geb_free_st = 0
        acc_free = 0
        g0_free_so = 0
        sq_free_st = [0, 0]
        pln_free_ref = [0]
        p.wait("gpsimd", "so", so_cw)
        for ts2 in range(NT):
            buf = ts2 % 2
            p.wait("sync", "st", xnt_free_st[buf])
            p.wait("sync", "sv", xnt_free_sv[buf])
            src = ap_of(xn0, ts2 * W, [[TOK, 128], [128 * TOK, 8], [1, W]])
            xv = xnt[:, buf * 8 * W:(buf + 1) * 8 * W].rearrange(
                "p (k w) -> p k w", k=8)
            sx_t = p.dma("sync", xv, src, "sx")
            p.wait("vector", "sv", acc_free)
            sv_z = p.op("vector", "memset", acc[:], 0.0, sem="sv")
            for e in range(8):
                ebuf = e % 2
                p.wait("gpsimd", "sv", cwr_free[ebuf])
                sw_cw = p.dma("gpsimd", cwr[:, ebuf * W:(ebuf + 1) * W],
                              ap_of(cwdram, e * TOK + ts2 * W, [[0, 128], [1, W]]),
                              "sw")
                sw_wu_arr = {}

                def dma_wu(mfg):
                    gbuf = mfg % 2
                    p.wait("gpsimd", "st", wu_free[gbuf])
                    sw_wu_arr[mfg] = p.dma(
                        "gpsimd",
                        wu[:, gbuf * 4096:(gbuf + 1) * 4096].rearrange(
                            "p (k c) -> p k c", c=512),
                        ap_of(w1t, e * DM * 4096 + mfg * 512,
                              [[4096, 128], [128 * 4096, 8], [1, 512]]), "sw")
                dma_wu(0)
                dma_wu(1)
                for mf in range(32):
                    mfg, mfl = mf // 4, mf % 4
                    gbuf = mfg % 2
                    mbuf = mf % 2
                    bank = pB[:, mbuf * 512:mbuf * 512 + W]
                    p.wait("tensor", "sw", sw_wu_arr[mfg])
                    p.wait("tensor", "sa", upbank_free[mbuf])
                    if mf == 0:
                        p.wait("tensor", "sx", sx_t)
                        p.wait("tensor", "sv", sv_z)
                    for k in range(8):
                        p.op("tensor", "matmul", bank,
                             wu[:, gbuf * 4096 + k * 512 + mfl * 128:
                                gbuf * 4096 + k * 512 + (mfl + 1) * 128],
                             xnt[:, buf * 8 * W + k * W:buf * 8 * W + (k + 1) * W],
                             start=(k == 0), stop=(k == 7),
                             sem=("st" if k == 7 else None))
                    st_up = p.count("st")
                    if mfl == 3:
                        wu_free[gbuf] = st_up
                        if mfg + 2 < 8:
                            dma_wu(mfg + 2)
                    p.wait("scalar", "st", st_up)
                    upbank_free[mbuf] = p.op(
                        "scalar", "activation", geb[:, mf * W:(mf + 1) * W], bank,
                        AF.Gelu, bias=b1s[:, e * 32 + mf:e * 32 + mf + 1], sem="sa")
                gelu_done = p.count("sa")
                sw_wd_arr = {}

                def dma_wd(mhg):
                    gbuf = mhg % 2
                    p.wait("sync", "st", wd_free[gbuf])
                    sw_wd_arr[mhg] = p.dma(
                        "sync",
                        wd[:, gbuf * 8192:(gbuf + 1) * 8192].rearrange(
                            "p (k c) -> p k c", c=256),
                        ap_of(w2t, e * 4096 * DM + mhg * 256,
                              [[DM, 128], [128 * DM, 32], [1, 256]]), "sx")
                dma_wd(0)
                dma_wd(1)
                for mh in range(8):
                    mhg, mhl = mh // 2, mh % 2
                    gbuf = mhg % 2
                    mbuf = mh % 2
                    bank = pB[:, (2 + mbuf) * 512:(2 + mbuf) * 512 + W]
                    p.wait("tensor", "sx", sw_wd_arr[mhg])
                    p.wait("tensor", "sa", gelu_done)
                    p.wait("tensor", "sv", dnbank_free[mbuf])
                    for kf in range(32):
                        p.op("tensor", "matmul", bank,
                             wd[:, gbuf * 8192 + kf * 256 + mhl * 128:
                                gbuf * 8192 + kf * 256 + (mhl + 1) * 128],
                             geb[:, kf * W:(kf + 1) * W],
                             start=(kf == 0), stop=(kf == 31),
                             sem=("st" if kf == 31 else None))
                    st_dn = p.count("st")
                    if mhl == 1:
                        wd_free[gbuf] = st_dn
                        if mhg + 2 < 4:
                            dma_wd(mhg + 2)
                    p.wait("vector", "st", st_dn)
                    if mh == 0:
                        p.wait("vector", "sw", sw_cw)
                    p.op("vector", "scalar_tensor_tensor", tmpv[:], bank,
                         b2s[:, e * 8 + mh:e * 8 + mh + 1],
                         cwr[:, ebuf * W:(ebuf + 1) * W], ALU.add, ALU.mult)
                    dnbank_free[mbuf] = p.op(
                        "vector", "tensor_add", acc[:, mh * W:(mh + 1) * W],
                        acc[:, mh * W:(mh + 1) * W], tmpv[:], sem="sv")
                cwr_free[ebuf] = p.count("sv")
                if DEBUG_MOE and ts2 == 0 and e == 0:
                    p.wait("gpsimd", "sv", p.count("sv"))
                    p.wait("gpsimd", "sa", p.count("sa"))
                    p.dma("gpsimd", o_geb[:], geb[:], "so")
                    p.dma("gpsimd", o_acc[:], acc[:], "so")
                    p.wait("scalar", "so", p.count("so"))
                    p.wait("vector", "so", p.count("so"))
            geb_free_st = p.count("st")
            p.wait("vector", "sx", sx_t)
            for k in range(8):
                p.op("vector", "tensor_add", acc[:, k * W:(k + 1) * W],
                     acc[:, k * W:(k + 1) * W],
                     xnt[:, buf * 8 * W + k * W:buf * 8 * W + (k + 1) * W],
                     sem=("sv" if k == 7 else None))
            res_sv = p.count("sv")
            xnt_free_sv[buf] = res_sv
            p.wait("scalar", "sv", res_sv)
            srcs = [acc[:, k * W:(k + 1) * W] for k in range(8)]
            mean_ap, rstd_ap, st_bc = emit_ln_fm(
                p, 8, W, srcs, pA[:, 0:1024], pA[:, 1024:2048], sqbuf, rows,
                ones_c, ones_r, epsc[0:1, 0:1], DM, sq_free_st, pln_free_ref,
                pbx_free_sv=acc_free)
            p.wait("vector", "st", st_bc)
            p.wait("vector", "so", g0_free_so)
            gk_sa = [0, 0]
            for k in range(8):
                kb = k % 2
                g1v = gsc[:, kb * W:(kb + 1) * W]
                g2v = gsc2[:, kb * W:(kb + 1) * W]
                p.wait("vector", "sa", gk_sa[kb])
                p.op("vector", "tensor_sub", g1v, acc[:, k * W:(k + 1) * W],
                     mean_ap)
                p.op("vector", "tensor_mul", g2v, g1v, rstd_ap)
                sv_g = p.op("vector", "tensor_scalar", g1v, g2v,
                            lnw[:, k:k + 1], lnb[:, k:k + 1], ALU.mult, ALU.add,
                            sem="sv")
                p.wait("scalar", "sv", sv_g)
                gk_sa[kb] = p.op("scalar", "activation",
                                 g0t[:, k * W:(k + 1) * W], g1v,
                                 AF.Silu, sem="sa")
            acc_free = p.count("sv")
            xnt_free_st[buf] = p.count("st")
            sa_g0 = p.count("sa")
            p.wait("gpsimd", "sa", sa_g0)
            dst = ap_of(o_g0, ts2 * W, [[TOK, 128], [128 * TOK, 8], [1, W]])
            g0_free_so = p.dma("gpsimd", dst,
                               g0t[:].rearrange("p (k w) -> p k w", k=8), "so")
        finish(nc, p)
    return nc


def run_moe(inputs, xn0, wres=None):
    if wres is None:
        wres = run_wcast(inputs)
    b1 = inputs["L0_ffn_b1"]
    b2 = inputs["L0_ffn_b2"]
    b1cc = np.ascontiguousarray(
        b1.reshape(8, 32, 128).transpose(2, 0, 1).reshape(128, 256))
    b2cc = np.ascontiguousarray(
        b2.reshape(8, 8, 128).transpose(2, 0, 1).reshape(128, 64))
    rwtt = np.ascontiguousarray(inputs["L0_ffn_router_w"].T)
    rbrr = inputs["L0_ffn_router_b"].reshape(1, 8).astype(np.float32)
    lnwcc = np.ascontiguousarray(inputs["L0_ffn_ln_w"].reshape(8, 128).T)
    lnbcc = np.ascontiguousarray(inputs["L0_ffn_ln_b"].reshape(8, 128).T)
    eyein = np.eye(128, dtype=np.float32)

    nc = _get("moe", build_moe)
    in_maps = []
    for core in range(NC):
        b, h = divmod(core, 2)
        t0 = b * 8192 + h * TOK
        in_maps.append({
            "xn0": np.ascontiguousarray(xn0[t0:t0 + TOK, :].T),
            "w1t": wres[core]["w1b"].reshape(8, DM, 4096),
            "w2t": wres[core]["w2b"].reshape(8, 4096, DM),
            "b1c": b1cc, "b2c": b2cc,
            "rwt": rwtt, "rbr": rbrr, "lnwc": lnwcc, "lnbc": lnbcc,
            "eyein": eyein,
        })
    res = run_bass_kernel_spmd(nc, in_maps, core_ids=list(range(NC))).results
    g0 = np.empty((TT32, DM), dtype=BF)
    for core in range(NC):
        b, h = divmod(core, 2)
        t0 = b * 8192 + h * TOK
        g0[t0:t0 + TOK, :] = res[core]["g0"].T
    return g0


def build_comb0():
    nc = bass.Bass("TRN2", target_bir_lowering=False)
    yT = nc.dram_tensor("yT", [DI, TOK], BF16, kind="ExternalInput").ap()
    zT = nc.dram_tensor("zT", [DI, TOK], BF16, kind="ExternalInput").ap()
    xT = nc.dram_tensor("xT", [DM, TOK], F32, kind="ExternalInput").ap()
    g0 = nc.dram_tensor("g0", [DM, TOK], BF16, kind="ExternalInput").ap()
    wop = nc.dram_tensor("wop", [DI, DM], F32, kind="ExternalInput").ap()
    o_x1 = nc.dram_tensor("x1", [DM, TOK], F32, kind="ExternalOutput").ap()

    p = Prog()
    W = 512
    with contextlib.ExitStack() as st_:
        def sb(name, shape, dt=F32):
            return st_.enter_context(nc.sbuf_tensor(name, shape, dt))

        wo = sb("wo", [128, 16 * 1024], BF16)
        yt = sb("yt", [128, 2 * 16 * W], BF16)
        zt = sb("zt", [128, 2 * 16 * W], BF16)
        xt = sb("xt", [128, 2 * 8 * W])
        g0t = sb("g0t", [128, 2 * 8 * W], BF16)
        sz = sb("sz", [128, 2 * W], BF16)
        yz = sb("yz", [128, 16 * W], BF16)
        sg = sb("sg", [128, W])
        x1b = sb("x1b", [128, 8 * W])
        pC = st_.enter_context(nc.psum_tensor("pC", [128, 2 * 512], F32))

        for k in range(16):
            p.dma("gpsimd", wo[:, k * 1024:(k + 1) * 1024],
                  wop[k * 128:(k + 1) * 128, :], "sw")
        wload = p.count("sw")

        in_free = [0, 0]
        yz_free_st = 0
        bank_free = [0, 0]
        x1_free_so = 0
        for t in range(NT):
            buf = t % 2
            p.wait("sync", "sv", in_free[buf])
            sx1 = p.dma("sync", yt[:, buf * 16 * W:(buf + 1) * 16 * W].rearrange(
                "p (k w) -> p k w", k=16),
                ap_of(yT, t * W, [[TOK, 128], [128 * TOK, 16], [1, W]]), "sx")
            sx2 = p.dma("sync", zt[:, buf * 16 * W:(buf + 1) * 16 * W].rearrange(
                "p (k w) -> p k w", k=16),
                ap_of(zT, t * W, [[TOK, 128], [128 * TOK, 16], [1, W]]), "sx")
            sx3 = p.dma("sync", xt[:, buf * 8 * W:(buf + 1) * 8 * W].rearrange(
                "p (k w) -> p k w", k=8),
                ap_of(xT, t * W, [[TOK, 128], [128 * TOK, 8], [1, W]]), "sx")
            sx4 = p.dma("sync", g0t[:, buf * 8 * W:(buf + 1) * 8 * W].rearrange(
                "p (k w) -> p k w", k=8),
                ap_of(g0, t * W, [[TOK, 128], [128 * TOK, 8], [1, W]]), "sx")
            p.wait("scalar", "sx", sx2)
            p.wait("vector", "sx", sx1)
            p.wait("vector", "st", yz_free_st)
            sz_free_sv = [0, 0]
            for k in range(16):
                kb = k % 2
                szk = sz[:, kb * W:(kb + 1) * W]
                p.wait("scalar", "sv", sz_free_sv[kb])
                sa_sz = p.op("scalar", "activation", szk,
                             zt[:, buf * 16 * W + k * W:buf * 16 * W + (k + 1) * W],
                             AF.Silu, sem="sa")
                p.wait("vector", "sa", sa_sz)
                sz_free_sv[kb] = p.op(
                    "vector", "tensor_mul", yz[:, k * W:(k + 1) * W],
                    yt[:, buf * 16 * W + k * W:buf * 16 * W + (k + 1) * W],
                    szk, sem="sv")
            yz_sv = p.count("sv")
            p.wait("tensor", "sv", yz_sv)
            if t == 0:
                p.wait("tensor", "sw", wload)
            p.wait("vector", "sx", sx3)
            p.wait("vector", "sx", sx4)
            p.wait("vector", "so", x1_free_so)
            for m in range(8):
                mb = m % 2
                bank = pC[:, mb * 512:mb * 512 + W]
                p.wait("tensor", "sv", bank_free[mb])
                for k in range(16):
                    p.op("tensor", "matmul", bank,
                         wo[:, k * 1024 + m * 128:k * 1024 + (m + 1) * 128],
                         yz[:, k * W:(k + 1) * W], start=(k == 0), stop=(k == 15),
                         sem=("st" if k == 15 else None))
                st_m = p.count("st")
                p.wait("vector", "st", st_m)
                p.op("vector", "tensor_mul", sg[:], bank,
                     g0t[:, buf * 8 * W + m * W:buf * 8 * W + (m + 1) * W])
                bank_free[mb] = p.op(
                    "vector", "tensor_add", x1b[:, m * W:(m + 1) * W],
                    xt[:, buf * 8 * W + m * W:buf * 8 * W + (m + 1) * W], sg[:],
                    sem="sv")
            yz_free_st = p.count("st")
            x1_sv = p.count("sv")
            in_free[buf] = x1_sv
            p.wait("gpsimd", "sv", x1_sv)
            x1_free_so = p.dma("gpsimd",
                               ap_of(o_x1, t * W, [[TOK, 128], [128 * TOK, 8], [1, W]]),
                               x1b[:].rearrange("p (k w) -> p k w", k=8), "so")
        finish(nc, p)
    return nc


def run_comb0(inputs, y, z, x, g0):
    wopt = np.ascontiguousarray(inputs["L0_attn_out_proj_w"].T)
    nc = _get("comb0", build_comb0)
    in_maps = []
    for core in range(NC):
        b, h = divmod(core, 2)
        t0 = b * 8192 + h * TOK
        sl = slice(t0, t0 + TOK)
        in_maps.append({
            "yT": np.ascontiguousarray(y[sl].T),
            "zT": np.ascontiguousarray(z[sl].T),
            "xT": np.ascontiguousarray(x[sl].T.astype(np.float32)),
            "g0": np.ascontiguousarray(g0[sl].T),
            "wop": wopt,
        })
    res = run_bass_kernel_spmd(nc, in_maps, core_ids=list(range(NC))).results
    x1 = np.empty((TT32, DM), np.float32)
    for core in range(NC):
        b, h = divmod(core, 2)
        t0 = b * 8192 + h * TOK
        x1[t0:t0 + TOK, :] = res[core]["x1"].T
    return x1


def build_l1pre():
    nc = bass.Bass("TRN2", target_bir_lowering=False)
    x1 = nc.dram_tensor("x1", [DM, TOK], F32, kind="ExternalInput").ap()
    win = nc.dram_tensor("win", [DM, DI], F32, kind="ExternalInput").ap()
    binc = nc.dram_tensor("binc", [128, 16], F32, kind="ExternalInput").ap()
    o_xn1 = nc.dram_tensor("xn1", [DM, TOK], F32, kind="ExternalOutput").ap()
    o_zab = nc.dram_tensor("zab", [DI, TOK], BF16, kind="ExternalOutput").ap()

    p = Prog()
    W = 512
    with contextlib.ExitStack() as st_:
        def sb(name, shape, dt=F32):
            return st_.enter_context(nc.sbuf_tensor(name, shape, dt))

        wi = sb("wi", [128, 8 * DI], BF16)
        bic = sb("bic", [128, 16])
        ones_c = sb("ones_c", [128, 1])
        ones_r = sb("ones_r", [1, 128])
        epsc = sb("epsc", [128, 1])
        xt = sb("xt", [128, 2 * 8 * W])
        sqbuf = sb("sqbuf", [128, 2 * W])
        rows = sb("rows", [1, 4 * W])
        xnf = sb("xnf", [128, 8 * W])
        xnb = sb("xnb", [128, 8 * W], BF16)
        zb_ = sb("zb_", [128, 2 * W], BF16)
        psq = st_.enter_context(nc.psum_tensor("psq", [128, 2 * 512], F32))
        pbx = st_.enter_context(nc.psum_tensor("pbx", [128, 2 * 512], F32))
        pmm = st_.enter_context(nc.psum_tensor("pmm", [128, 2 * 512], F32))

        for k in range(8):
            p.dma("gpsimd", wi[:, k * DI:(k + 1) * DI],
                  win[k * 128:(k + 1) * 128, :], "sw")
        p.dma("sync", bic[:], binc[:], "sx")
        wload = p.count("sw")
        st_sx = p.count("sx")
        p.op("vector", "memset", ones_c[:], 1.0)
        p.op("vector", "memset", epsc[:], 1e-5)
        p.op("vector", "memset", ones_r[:], 1.0, sem="sv")
        ms_sv = p.count("sv")

        xt_free = [0, 0]
        sq_free_st = [0, 0]
        pln_free_ref = [0]
        z_free_so = [0, 0]
        bank_free = [0, 0]
        xn_free = 0
        xn_so = 0
        for t in range(NT):
            buf = t % 2
            p.wait("sync", "sv", xt_free[buf])
            sx_t = p.dma("sync", xt[:, buf * 8 * W:(buf + 1) * 8 * W].rearrange(
                "p (k w) -> p k w", k=8),
                ap_of(x1, t * W, [[TOK, 128], [128 * TOK, 8], [1, W]]), "sx")
            p.wait("scalar", "sx", sx_t)
            p.wait("tensor", "sx", sx_t)
            if t == 0:
                p.wait("tensor", "sv", ms_sv)
                p.wait("scalar", "sv", ms_sv)
            srcs = [xt[:, buf * 8 * W + k * W:buf * 8 * W + (k + 1) * W]
                    for k in range(8)]
            mean_ap, rstd_ap, st_bc = emit_ln_fm(
                p, 8, W, srcs, psq, pbx, sqbuf, rows, ones_c, ones_r,
                epsc[0:1, 0:1], DM, sq_free_st, pln_free_ref, pbx_free_sv=xn_free)
            p.wait("vector", "st", st_bc)
            p.wait("vector", "so", xn_so)
            for k in range(8):
                p.op("vector", "tensor_sub", xnf[:, k * W:(k + 1) * W], srcs[k],
                     mean_ap)
                sv_xn = p.op("vector", "tensor_mul", xnb[:, k * W:(k + 1) * W],
                             xnf[:, k * W:(k + 1) * W], rstd_ap, sem="sv")
            for k in range(8):
                sv_xn = p.op("vector", "tensor_mul", xnf[:, k * W:(k + 1) * W],
                             xnf[:, k * W:(k + 1) * W], rstd_ap, sem="sv")
            xn_free = sv_xn
            xt_free[buf] = sv_xn
            p.wait("gpsimd", "sv", sv_xn)
            xn_so = p.dma("gpsimd",
                          ap_of(o_xn1, t * W, [[TOK, 128], [128 * TOK, 8], [1, W]]),
                          xnf[:].rearrange("p (k w) -> p k w", k=8), "so")
            p.wait("tensor", "sv", sv_xn)
            if t == 0:
                p.wait("tensor", "sw", wload)
            for m in range(16):
                mb = m % 2
                bank = pmm[:, mb * 512:mb * 512 + W]
                p.wait("tensor", "sv", bank_free[mb])
                for k in range(8):
                    p.op("tensor", "matmul", bank,
                         wi[:, k * DI + m * 128:k * DI + (m + 1) * 128],
                         xnb[:, k * W:(k + 1) * W], start=(k == 0), stop=(k == 7),
                         sem=("st" if k == 7 else None))
                st_m = p.count("st")
                p.wait("vector", "st", st_m)
                p.wait("vector", "so", z_free_so[mb])
                zv = zb_[:, mb * W:(mb + 1) * W]
                bank_free[mb] = p.op("vector", "tensor_scalar", zv, bank,
                                     bic[:, m:m + 1], None, ALU.add, sem="sv")
                p.wait("gpsimd", "sv", bank_free[mb])
                z_free_so[mb] = p.dma(
                    "gpsimd", ap_of(o_zab, m * 128 * TOK + t * W, [[TOK, 128], [1, W]]),
                    zv, "so")
        finish(nc, p)
    return nc


def run_l1pre(inputs, x1):
    wint = np.ascontiguousarray(inputs["L1_attn_w_in"].T)
    bincc = np.ascontiguousarray(inputs["L1_attn_b_in"].reshape(16, 128).T)
    nc = _get("l1pre", build_l1pre)
    in_maps = []
    for core in range(NC):
        b, h = divmod(core, 2)
        t0 = b * 8192 + h * TOK
        in_maps.append({"x1": np.ascontiguousarray(x1[t0:t0 + TOK].T),
                        "win": wint, "binc": bincc})
    res = run_bass_kernel_spmd(nc, in_maps, core_ids=list(range(NC))).results
    xn1 = np.empty((TT32, DM), np.float32)
    zab = np.empty((TT32, DI), dtype=BF)
    for core in range(NC):
        b, h = divmod(core, 2)
        t0 = b * 8192 + h * TOK
        xn1[t0:t0 + TOK] = res[core]["xn1"].T
        zab[t0:t0 + TOK] = res[core]["zab"].T
    return xn1, zab


NTIL = 64


def build_conv():
    nc = bass.Bass("TRN2", target_bir_lowering=False)
    xa = nc.dram_tensor("xa", [128, 128 * 4 * 66], BF16, kind="ExternalInput").ap()
    xb = nc.dram_tensor("xb", [128, 128 * 4 * 66], BF16, kind="ExternalInput").ap()
    ta = nc.dram_tensor("ta", [128 * 3 * 128, 128], F32, kind="ExternalInput").ap()
    tb = nc.dram_tensor("tb", [128 * 3 * 128, 128], F32, kind="ExternalInput").ap()
    bcols = nc.dram_tensor("bcols", [128, 2 * 128], F32, kind="ExternalInput").ap()
    o_ga = nc.dram_tensor("ga", [128, 128 * 4 * 64], BF16, kind="ExternalOutput").ap()
    o_gb = nc.dram_tensor("gb", [128, 128 * 4 * 64], BF16, kind="ExternalOutput").ap()

    p = Prog()
    with contextlib.ExitStack() as st_:
        def sb(name, shape, dt=F32):
            return st_.enter_context(nc.sbuf_tensor(name, shape, dt))

        xas = sb("xas", [128, 128 * 4 * 66], BF16)
        xbs = sb("xbs", [128, 128 * 4 * 66], BF16)
        tw = sb("tw", [128, 2 * 2 * 3 * 128], BF16)
        bc = sb("bc", [128, 2 * 128])
        ca = sb("ca", [128, 256])
        cb_ = sb("cb_", [128, 256])
        sga = sb("sga", [128, 256])
        sgb = sb("sgb", [128, 256])
        ga = sb("gax", [128, 2 * 256], BF16)
        gb = sb("gbx", [128, 2 * 256], BF16)
        pcv = st_.enter_context(nc.psum_tensor("pcv", [128, 4 * 512], F32))

        sxa = p.dma("sync", xas[:], xa[:], "sx")
        sxb = p.dma("sync", xbs[:], xb[:], "sx")
        p.dma("sync", bc[:], bcols[:], "sx")
        in_sx = p.count("sx")

        tw_free = [0, 0]
        bank_free = [0, 0]
        g_free = [0, 0]
        sg_free = [0, 0]
        prev_sv = 0
        prev_sa_sig = 0
        for ch in range(128):
            cbuf = ch % 2
            p.wait("gpsimd", "st", tw_free[cbuf])
            twoff = cbuf * 2 * 3 * 128
            p.dma("gpsimd", tw[:, twoff:twoff + 384].rearrange(
                "p (d c) -> p d c", c=128),
                ap_of(ta, ch * 3 * 128 * 128, [[128, 128], [128 * 128, 3], [1, 128]]),
                "sw")
            sw_t = p.dma("gpsimd", tw[:, twoff + 384:twoff + 768].rearrange(
                "p (d c) -> p d c", c=128),
                ap_of(tb, ch * 3 * 128 * 128, [[128, 128], [128 * 128, 3], [1, 128]]),
                "sw")
            p.wait("tensor", "sw", sw_t)
            if ch == 0:
                p.wait("tensor", "sx", in_sx)
            p.wait("tensor", "sv", bank_free[cbuf])
            bka = pcv[:, cbuf * 1024:cbuf * 1024 + 256]
            bkb = pcv[:, cbuf * 1024 + 512:cbuf * 1024 + 768]
            for cv in range(2):
                xs = xas if cv == 0 else xbs
                bk = bka if cv == 0 else bkb
                for b in range(4):
                    xoff = ch * 4 * 66 + b * 66
                    reg = bk[:, b * 64:(b + 1) * 64]
                    for d in range(3):
                        p.op("tensor", "matmul", reg,
                             tw[:, twoff + cv * 384 + d * 128:
                                twoff + cv * 384 + (d + 1) * 128],
                             xs[:, xoff + (2 - d) * 1:xoff + (2 - d) + 64],
                             start=(d == 0), stop=(d == 2),
                             sem=("st" if (b == 3 and d == 2) else None))
            st_cv = p.count("st")
            tw_free[cbuf] = st_cv
            p.wait("vector", "st", st_cv)
            p.op("vector", "tensor_scalar", ca[:], bka, bc[:, ch:ch + 1], None,
                 ALU.add)
            sv_ev = p.op("vector", "tensor_scalar", cb_[:], bkb,
                         bc[:, 128 + ch:128 + ch + 1], None, ALU.add, sem="sv")
            bank_free[cbuf] = sv_ev
            p.wait("scalar", "sv", sv_ev)
            p.op("scalar", "activation", sga[:], ca[:], AF.Sigmoid)
            prev_sa_sig = p.op("scalar", "activation", sgb[:], cb_[:], AF.Sigmoid,
                               sem="sa")
            p.wait("vector", "sa", prev_sa_sig)
            p.wait("vector", "so", g_free[cbuf])
            p.op("vector", "tensor_mul", ga[:, cbuf * 256:(cbuf + 1) * 256],
                 ca[:], sgb[:])
            prev_sv = p.op("vector", "tensor_mul", gb[:, cbuf * 256:(cbuf + 1) * 256],
                           cb_[:], sga[:], sem="sv")
            p.wait("gpsimd", "sv", prev_sv)
            p.dma("gpsimd", ap_of(o_ga, ch * 256, [[128 * 4 * 64, 128], [1, 256]]),
                  ga[:, cbuf * 256:(cbuf + 1) * 256], "so")
            g_free[cbuf] = p.dma(
                "gpsimd", ap_of(o_gb, ch * 256, [[128 * 4 * 64, 128], [1, 256]]),
                gb[:, cbuf * 256:(cbuf + 1) * 256], "so")
        finish(nc, p)
    return nc


def _toeplitz(wv):
    nch = wv.shape[0]
    v = wv[:, ::-1]
    T = np.zeros((nch, 3, 128, 128), np.float32)
    pp, mm = np.meshgrid(np.arange(128), np.arange(128), indexing="ij")
    for d in range(3):
        j = mm - pp + 128 * d
        val = (j >= 0) & (j <= 255)
        jc = np.clip(j, 0, 255)
        T[:, d] = np.where(val[None, :, :], v[:, jc], 0.0)
    return T


def run_conv(inputs, zab):
    za = zab[:, :1024].astype(BF)
    zb = zab[:, 1024:].astype(BF)
    wa = inputs["L1_attn_w_a"].reshape(1024, 256)
    wb = inputs["L1_attn_w_b"].reshape(1024, 256)
    Ta = _toeplitz(wa)
    Tb = _toeplitz(wb)
    ba = inputs["L1_attn_b_a"]
    bb = inputs["L1_attn_b_b"]

    def xlayout(z, c0):
        zc = z[:, c0:c0 + 128].astype(BF)
        zt = zc.T.reshape(128, 4, 64, 128)
        X = np.zeros((128, 128, 4, 66), dtype=BF)
        X[:, :, :, 2:] = zt.transpose(3, 0, 1, 2)
        return np.ascontiguousarray(X.reshape(128, 128 * 4 * 66))

    nc = _get("conv", build_conv)
    in_maps = []
    for core in range(NC):
        c0 = core * 128
        bcol = np.zeros((128, 256), np.float32)
        bcol[:, 0:128] = ba[c0:c0 + 128][None, :]
        bcol[:, 128:256] = bb[c0:c0 + 128][None, :]
        in_maps.append({
            "xa": xlayout(za, c0), "xb": xlayout(zb, c0),
            "ta": np.ascontiguousarray(Ta[c0:c0 + 128].reshape(-1, 128)),
            "tb": np.ascontiguousarray(Tb[c0:c0 + 128].reshape(-1, 128)),
            "bcols": bcol,
        })
    res = run_bass_kernel_spmd(nc, in_maps, core_ids=list(range(NC))).results
    ga = np.empty((TT32, 1024), dtype=BF)
    gb = np.empty((TT32, 1024), dtype=BF)
    for core in range(NC):
        c0 = core * 128
        for name, dst in (("ga", ga), ("gb", gb)):
            r = res[core][name].reshape(128, 128, 4, 64)
            dst[:, c0:c0 + 128] = r.transpose(2, 3, 0, 1).reshape(TT32, 128)
    return ga, gb


def build_quant(name, nrows, width):
    nc = bass.Bass("TRN2", target_bir_lowering=False)
    xin = nc.dram_tensor("xin", [nrows, width], F32, kind="ExternalInput").ap()
    o_q = nc.dram_tensor("q", [nrows, width], BF16, kind="ExternalOutput").ap()
    nt = nrows // 128
    nb = width // 16

    p = Prog()
    with contextlib.ExitStack() as st_:
        def sb(name_, shape, dt=F32):
            return st_.enter_context(nc.sbuf_tensor(name_, shape, dt))

        xt = sb("xt", [128, 2 * width])
        bm = sb("bm", [128, nb])
        inv = sb("inv", [128, nb])
        tv = sb("tv", [128, width])
        uv = sb("uv", [128, width])
        ki = sb("ki", [128, width], I32)
        kf = sb("kf", [128, width])
        qv = sb("qv", [128, width])
        sgn = sb("sgn", [128, width])
        ssv = sb("ssv", [128, width])
        qo = sb("qo", [128, 2 * width], BF16)

        in_free = [0, 0]
        out_free = [0, 0]
        for t in range(nt):
            buf = t % 2
            p.wait("sync", "sv", in_free[buf])
            sx = p.dma("sync", xt[:, buf * width:(buf + 1) * width],
                       xin[t * 128:(t + 1) * 128, :], "sx")
            xv = xt[:, buf * width:(buf + 1) * width]
            p.wait("vector", "sx", sx)
            c = p.op("vector", "reduce_max", bm[:].unsqueeze(2),
                     xv.rearrange("p (b k) -> p b k", k=16),
                     axis=mybir.AxisListType.X, apply_absolute_value=True,
                     sem="sv")
            p.wait("vector", "sv", c)
            c = p.op("vector", "tensor_scalar", bm[:], bm[:], 1.0 / 6.0, 1e-12,
                     ALU.mult, ALU.max, sem="sv")
            p.wait("vector", "sv", c)
            c = p.op("vector", "reciprocal", inv[:], bm[:], sem="sv")
            p.wait("vector", "sv", c)
            invb = inv[:].unsqueeze(2).broadcast_to([128, nb, 16])
            bmb = bm[:].unsqueeze(2).broadcast_to([128, nb, 16])
            sv_tv = p.op("vector", "tensor_tensor", tv[:].rearrange(
                "p (b k) -> p b k", k=16), xv.rearrange("p (b k) -> p b k", k=16),
                invb, ALU.mult, sem="sv")
            p.wait("scalar", "sv", sv_tv)
            p.op("scalar", "activation", uv[:], tv[:], AF.Abs)
            sa_s = p.op("scalar", "activation", sgn[:], tv[:], AF.Sign, sem="sa")
            p.wait("vector", "sa", sa_s)
            p.op("vector", "tensor_scalar", ki[:], uv[:], 2.0, None, ALU.mult)
            p.op("vector", "tensor_copy", kf[:], ki[:])
            p.op("vector", "tensor_scalar", qv[:], kf[:], 0.5, 2.0, ALU.mult,
                 ALU.min)
            p.op("vector", "scalar_tensor_tensor", qv[:], uv[:], 2.5, qv[:],
                 ALU.is_ge, ALU.add)
            p.op("vector", "scalar_tensor_tensor", qv[:], uv[:], 3.5, qv[:],
                 ALU.is_ge, ALU.add)
            p.op("vector", "scalar_tensor_tensor", qv[:], uv[:], 5.0, qv[:],
                 ALU.is_ge, ALU.add)
            p.op("vector", "scalar_tensor_tensor", qv[:], uv[:], 5.0, qv[:],
                 ALU.is_ge, ALU.add)
            p.op("vector", "tensor_tensor", ssv[:].rearrange(
                "p (b k) -> p b k", k=16), sgn[:].rearrange(
                "p (b k) -> p b k", k=16), bmb, ALU.mult)
            p.wait("vector", "so", out_free[buf])
            sv_o = p.op("vector", "tensor_mul", qo[:, buf * width:(buf + 1) * width],
                        qv[:], ssv[:], sem="sv")
            in_free[buf] = sv_o
            p.wait("gpsimd", "sv", sv_o)
            out_free[buf] = p.dma("gpsimd", o_q[t * 128:(t + 1) * 128, :],
                                  qo[:, buf * width:(buf + 1) * width], "so")
        finish(nc, p)
    return nc


def run_quant_tok(arr, key):
    width = arr.shape[1]
    nc = _get(f"quant_{key}_{width}", lambda: build_quant(key, TOK, width))
    in_maps = []
    for core in range(NC):
        b, h = divmod(core, 2)
        t0 = b * 8192 + h * TOK
        in_maps.append({"xin": np.ascontiguousarray(arr[t0:t0 + TOK])})
    res = run_bass_kernel_spmd(nc, in_maps, core_ids=list(range(NC))).results
    q = np.empty(arr.shape, dtype=BF)
    for core in range(NC):
        b, h = divmod(core, 2)
        t0 = b * 8192 + h * TOK
        q[t0:t0 + TOK] = res[core]["q"]
    return q


def run_quant_w(inputs):
    wg = inputs["L1_ffn_w_gate"].astype(np.float32)
    wu = inputs["L1_ffn_w_up"].astype(np.float32)
    wd = inputs["L1_ffn_w_down"].astype(np.float32)
    w1024 = np.concatenate([wg, wu, np.zeros((512, 1024), np.float32)], axis=0)
    nc1 = _get("quant_w1024", lambda: build_quant("w1024", 768, 1024))
    in_maps = [{"xin": np.ascontiguousarray(w1024[c * 768:(c + 1) * 768])}
               for c in range(NC)]
    res = run_bass_kernel_spmd(nc1, in_maps, core_ids=list(range(NC))).results
    q1024 = np.concatenate([res[c]["q"] for c in range(NC)], axis=0)
    wgq, wuq = q1024[:2816], q1024[2816:5632]
    nc2 = _get("quant_w2816", lambda: build_quant("w2816", 128, 2816))
    in_maps = [{"xin": np.ascontiguousarray(wd[c * 128:(c + 1) * 128])}
               for c in range(NC)]
    res = run_bass_kernel_spmd(nc2, in_maps, core_ids=list(range(NC))).results
    wdq = np.concatenate([res[c]["q"] for c in range(NC)], axis=0)
    return wgq, wuq, wdq


FFI = 2816
NM22 = FFI // 128


def build_ffn1():
    nc = bass.Bass("TRN2", target_bir_lowering=False)
    xq = nc.dram_tensor("xq", [DM, TOK], BF16, kind="ExternalInput").ap()
    wg = nc.dram_tensor("wg", [DM, FFI], BF16, kind="ExternalInput").ap()
    wu = nc.dram_tensor("wu", [DM, FFI], BF16, kind="ExternalInput").ap()
    o_h = nc.dram_tensor("h", [FFI, TOK], F32, kind="ExternalOutput").ap()

    p = Prog()
    W = 512
    with contextlib.ExitStack() as st_:
        def sb(name, shape, dt=F32):
            return st_.enter_context(nc.sbuf_tensor(name, shape, dt))

        wgs = sb("wgs", [128, 8 * FFI], BF16)
        wus = sb("wus", [128, 8 * FFI], BF16)
        xqs = sb("xqs", [128, 2 * 8 * W], BF16)
        sg = sb("sg", [128, 2 * W])
        hb = sb("hb", [128, 2 * W])
        pg = st_.enter_context(nc.psum_tensor("pg", [128, 2 * 512], F32))
        pu = st_.enter_context(nc.psum_tensor("pu", [128, 2 * 512], F32))

        for k in range(8):
            p.dma("gpsimd", wgs[:, k * FFI:(k + 1) * FFI],
                  wg[k * 128:(k + 1) * 128, :], "sw")
            p.dma("gpsimd", wus[:, k * FFI:(k + 1) * FFI],
                  wu[k * 128:(k + 1) * 128, :], "sw")
        wload = p.count("sw")
        in_free = [0, 0]
        gb_free = [0, 0]
        ub_free = [0, 0]
        h_free = [0, 0]
        sg_free = [0, 0]
        for t in range(NT):
            buf = t % 2
            p.wait("sync", "st", in_free[buf])
            sx = p.dma("sync", xqs[:, buf * 8 * W:(buf + 1) * 8 * W].rearrange(
                "p (k w) -> p k w", k=8),
                ap_of(xq, t * W, [[TOK, 128], [128 * TOK, 8], [1, W]]), "sx")
            p.wait("tensor", "sx", sx)
            if t == 0:
                p.wait("tensor", "sw", wload)
            for m in range(NM22):
                mb = m % 2
                bg = pg[:, mb * 512:mb * 512 + W]
                bu = pu[:, mb * 512:mb * 512 + W]
                p.wait("tensor", "sa", gb_free[mb])
                for k in range(8):
                    p.op("tensor", "matmul", bg,
                         wgs[:, k * FFI + m * 128:k * FFI + (m + 1) * 128],
                         xqs[:, buf * 8 * W + k * W:buf * 8 * W + (k + 1) * W],
                         start=(k == 0), stop=(k == 7))
                p.wait("tensor", "sv", ub_free[mb])
                for k in range(8):
                    p.op("tensor", "matmul", bu,
                         wus[:, k * FFI + m * 128:k * FFI + (m + 1) * 128],
                         xqs[:, buf * 8 * W + k * W:buf * 8 * W + (k + 1) * W],
                         start=(k == 0), stop=(k == 7),
                         sem=("st" if k == 7 else None))
                st_m = p.count("st")
                p.wait("scalar", "st", st_m)
                p.wait("scalar", "sv", sg_free[mb])
                gb_free[mb] = p.op("scalar", "activation", sg[:, mb * W:(mb + 1) * W],
                                   bg, AF.Silu, sem="sa")
                p.wait("vector", "sa", gb_free[mb])
                p.wait("vector", "so", h_free[mb])
                sg_free[mb] = p.op("vector", "tensor_mul", hb[:, mb * W:(mb + 1) * W],
                                   sg[:, mb * W:(mb + 1) * W], bu, sem="sv")
                ub_free[mb] = sg_free[mb]
                p.wait("gpsimd", "sv", sg_free[mb])
                h_free[mb] = p.dma(
                    "gpsimd", ap_of(o_h, m * 128 * TOK + t * W, [[TOK, 128], [1, W]]),
                    hb[:, mb * W:(mb + 1) * W], "so")
            in_free[buf] = p.count("st")
        finish(nc, p)
    return nc


def run_ffn1(xq_tok, wgq, wuq):
    wgT = np.ascontiguousarray(wgq.T)
    wuT = np.ascontiguousarray(wuq.T)
    nc = _get("ffn1", build_ffn1)
    in_maps = []
    for core in range(NC):
        b, h = divmod(core, 2)
        t0 = b * 8192 + h * TOK
        in_maps.append({"xq": np.ascontiguousarray(xq_tok[t0:t0 + TOK].T),
                        "wg": wgT, "wu": wuT})
    res = run_bass_kernel_spmd(nc, in_maps, core_ids=list(range(NC))).results
    hmid = np.empty((TT32, FFI), np.float32)
    for core in range(NC):
        b, h = divmod(core, 2)
        t0 = b * 8192 + h * TOK
        hmid[t0:t0 + TOK] = res[core]["h"].T
    return hmid


def build_comb1():
    nc = bass.Bass("TRN2", target_bir_lowering=False)
    hq = nc.dram_tensor("hq", [FFI, TOK], BF16, kind="ExternalInput").ap()
    wd = nc.dram_tensor("wd", [FFI, DM], BF16, kind="ExternalInput").ap()
    gc = nc.dram_tensor("gc", [DI, TOK], BF16, kind="ExternalInput").ap()
    wo = nc.dram_tensor("wo", [DI, DM], F32, kind="ExternalInput").ap()
    boc = nc.dram_tensor("boc", [128, 8], F32, kind="ExternalInput").ap()
    x1 = nc.dram_tensor("x1", [DM, TOK], F32, kind="ExternalInput").ap()
    o_x2 = nc.dram_tensor("x2", [DM, TOK], F32, kind="ExternalOutput").ap()

    p = Prog()
    W = 512
    with contextlib.ExitStack() as st_:
        def sb(name, shape, dt=F32):
            return st_.enter_context(nc.sbuf_tensor(name, shape, dt))

        wds = sb("wds", [128, NM22 * DM], BF16)
        wos = sb("wos", [128, 16 * DM], BF16)
        bo = sb("bo", [128, 8])
        hqs = sb("hqs", [128, NM22 * W], BF16)
        gcs = sb("gcs", [128, 2 * 16 * W], BF16)
        x1s = sb("x1s", [128, 2 * 8 * W])
        g1 = sb("g1", [128, 2 * W])
        t1 = sb("t1", [128, 2 * W])
        x2b = sb("x2b", [128, 8 * W])
        pf = st_.enter_context(nc.psum_tensor("pf", [128, 2 * 512], F32))
        ps = st_.enter_context(nc.psum_tensor("ps", [128, 2 * 512], F32))

        for k in range(NM22):
            p.dma("gpsimd", wds[:, k * DM:(k + 1) * DM],
                  wd[k * 128:(k + 1) * 128, :], "sw")
        for k in range(16):
            p.dma("gpsimd", wos[:, k * DM:(k + 1) * DM],
                  wo[k * 128:(k + 1) * 128, :], "sw")
        p.dma("sync", bo[:], boc[:], "sx")
        wload = p.count("sw")
        bo_sx = p.count("sx")

        in_free = [0, 0]
        pf_free = [0, 0]
        ps_free = [0, 0]
        g1_free = [0, 0]
        t1_free = [0, 0]
        x2_so = 0
        hq_free = 0
        for t in range(NT):
            buf = t % 2
            p.wait("sync", "st", in_free[buf])
            p.wait("sync", "st", hq_free)
            sx1 = p.dma("sync", hqs[:].rearrange("p (k w) -> p k w", k=NM22),
                        ap_of(hq, t * W, [[TOK, 128], [128 * TOK, NM22], [1, W]]),
                        "sx")
            sx2 = p.dma("sync", gcs[:, buf * 16 * W:(buf + 1) * 16 * W].rearrange(
                "p (k w) -> p k w", k=16),
                ap_of(gc, t * W, [[TOK, 128], [128 * TOK, 16], [1, W]]), "sx")
            sx3 = p.dma("sync", x1s[:, buf * 8 * W:(buf + 1) * 8 * W].rearrange(
                "p (k w) -> p k w", k=8),
                ap_of(x1, t * W, [[TOK, 128], [128 * TOK, 8], [1, W]]), "sx")
            p.wait("tensor", "sx", max(sx1, sx2))
            if t == 0:
                p.wait("tensor", "sw", wload)
                p.wait("scalar", "sx", bo_sx)
            p.wait("vector", "sx", sx3)
            p.wait("vector", "so", x2_so)
            for m in range(8):
                mb = m % 2
                bf_ = pf[:, mb * 512:mb * 512 + W]
                bs_ = ps[:, mb * 512:mb * 512 + W]
                p.wait("tensor", "sa", pf_free[mb])
                for k in range(NM22):
                    p.op("tensor", "matmul", bf_,
                         wds[:, k * DM + m * 128:k * DM + (m + 1) * 128],
                         hqs[:, k * W:(k + 1) * W],
                         start=(k == 0), stop=(k == NM22 - 1),
                         sem=("st" if (m == 7 and k == NM22 - 1) else None))
                if m == 7:
                    hq_free = p.count("st")
                p.wait("tensor", "sv", ps_free[mb])
                for k in range(16):
                    p.op("tensor", "matmul", bs_,
                         wos[:, k * DM + m * 128:k * DM + (m + 1) * 128],
                         gcs[:, buf * 16 * W + k * W:buf * 16 * W + (k + 1) * W],
                         start=(k == 0), stop=(k == 15),
                         sem=("st" if k == 15 else None))
                st_m = p.count("st")
                p.wait("scalar", "st", st_m)
                p.wait("scalar", "sv", g1_free[mb])
                pf_free[mb] = p.op("scalar", "activation", g1[:, mb * W:(mb + 1) * W],
                                   bf_, AF.Silu, sem="sa")
                p.wait("vector", "sa", pf_free[mb])
                g1_free[mb] = p.op("vector", "scalar_tensor_tensor",
                                   t1[:, mb * W:(mb + 1) * W], bs_,
                                   bo[:, m:m + 1], g1[:, mb * W:(mb + 1) * W],
                                   ALU.add, ALU.mult, sem="sv")
                ps_free[mb] = p.op("vector", "tensor_add",
                                   x2b[:, m * W:(m + 1) * W],
                                   x1s[:, buf * 8 * W + m * W:buf * 8 * W + (m + 1) * W],
                                   t1[:, mb * W:(mb + 1) * W], sem="sv")
                g1_free[mb] = ps_free[mb]
            in_free[buf] = p.count("st")
            x2_sv = p.count("sv")
            p.wait("gpsimd", "sv", x2_sv)
            x2_so = p.dma("gpsimd",
                          ap_of(o_x2, t * W, [[TOK, 128], [128 * TOK, 8], [1, W]]),
                          x2b[:].rearrange("p (k w) -> p k w", k=8), "so")
        finish(nc, p)
    return nc


def run_comb1(inputs, hq_tok, ga, gb, x1):
    wdT = None
    wopt = np.ascontiguousarray(inputs["L1_attn_w_out"].T)
    bocc = np.ascontiguousarray(inputs["L1_attn_b_out"].reshape(8, 128).T)
    gcat = np.concatenate([ga, gb], axis=1)
    nc = _get("comb1", build_comb1)
    in_maps = []
    for core in range(NC):
        b, h = divmod(core, 2)
        t0 = b * 8192 + h * TOK
        in_maps.append({
            "hq": np.ascontiguousarray(hq_tok[t0:t0 + TOK].T),
            "wd": _cache["wdqT"], "gc": np.ascontiguousarray(gcat[t0:t0 + TOK].T),
            "wo": wopt, "boc": bocc,
            "x1": np.ascontiguousarray(x1[t0:t0 + TOK].T),
        })
    res = run_bass_kernel_spmd(nc, in_maps, core_ids=list(range(NC))).results
    x2 = np.empty((TT32, DM), np.float32)
    for core in range(NC):
        b, h = divmod(core, 2)
        t0 = b * 8192 + h * TOK
        x2[t0:t0 + TOK] = res[core]["x2"].T
    return x2


_cache = {}


def _get(name, fn):
    if name not in _cache:
        _cache[name] = fn()
    return _cache[name]


def run_l0pre(inputs):
    x = inputs["L0_attn_x"]
    w_in = np.ascontiguousarray(inputs["L0_attn_in_proj_w"].T)
    xpw = inputs["L0_attn_x_proj_w"]
    w_xp = np.zeros((DI, 80), np.float32)
    w_xp[:, 0] = xpw[0]
    w_xp[:, 32:48] = xpw[1:17].T
    w_xp[:, 64:80] = xpw[17:33].T
    cwr = inputs["L0_attn_conv_w"].reshape(DI, 4)
    convw = np.ascontiguousarray(
        cwr.reshape(16, 128, 4).transpose(1, 0, 2).reshape(128, 64))
    convb = np.ascontiguousarray(inputs["L0_attn_conv_b"].reshape(16, 128).T)
    dtw = np.ascontiguousarray(inputs["L0_attn_dt_w"][:, 0].reshape(16, 128).T)
    dtb = np.ascontiguousarray(inputs["L0_attn_dt_b"].reshape(16, 128).T)

    nc = _get("l0pre", build_l0pre)
    in_maps = []
    for core in range(NC):
        b, h = divmod(core, 2)
        t0 = h * TOK
        xT = np.zeros((DM, TW + TOK), np.float32)
        xT[:, TW:] = x[b, t0:t0 + TOK, :].T
        if h == 1:
            xT[:, 0:TW] = x[b, t0 - TW:t0, :].T
        in_maps.append({"x": np.ascontiguousarray(xT), "w_in": w_in, "w_xp": w_xp,
                        "convw": convw, "convb": convb, "dtw": dtw, "dtb": dtb})
    res = run_bass_kernel_spmd(nc, in_maps, core_ids=list(range(NC)))
    return res.results


def kernel(**inputs):
    inputs = {k: np.asarray(v) for k, v in inputs.items()}
    r1 = run_l0pre(inputs)
    xn0 = np.empty((TT32, DM), dtype=BF)
    xcv = np.empty((TT32, DI), dtype=BF)
    zv = np.empty((TT32, DI), dtype=BF)
    delta = np.empty((TT32, DI), np.float32)
    Bp = np.empty((TT32, S16), np.float32)
    Cp = np.empty((TT32, S16), np.float32)
    for core in range(NC):
        b, h = divmod(core, 2)
        t0 = b * 8192 + h * TOK
        sl = slice(t0, t0 + TOK)
        xn0[sl] = r1[core]["xn0"].T
        xcv[sl] = r1[core]["xc"].T
        zv[sl] = r1[core]["z"].T
        delta[sl] = r1[core]["delta"].T
        Bp[sl] = r1[core]["Bmat"].T
        Cp[sl] = r1[core]["Cmat"].T
    wres = run_wcast(inputs)
    y = run_scan(inputs, delta, xcv, Bp, Cp)
    g0 = run_moe(inputs, xn0, wres)
    x = inputs["L0_attn_x"].reshape(TT32, DM)
    x1 = run_comb0(inputs, y, zv, x, g0)
    xn1, zab = run_l1pre(inputs, x1)
    ga, gb = run_conv(inputs, zab)
    wgq, wuq, wdq = run_quant_w(inputs)
    _cache["wdqT"] = np.ascontiguousarray(np.asarray(wdq).T)
    xq_tok = run_quant_tok(xn1, "xn1")
    hmid = run_ffn1(xq_tok, wgq, wuq)
    hq_tok = run_quant_tok(hmid, "h")
    x2 = run_comb1(inputs, hq_tok, ga, gb, x1)
    return x2.reshape(4, 8192, DM)
